# version 06
# baseline (speedup 1.0000x reference)
import numpy as np
import concourse.bass as bass
import concourse.mybir as mybir
from concourse.bass_utils import run_bass_kernel_spmd
from contextlib import ExitStack

F32 = mybir.dt.float32
BF16 = mybir.dt.bfloat16
AF = mybir.ActivationFunctionType
ALU = mybir.AluOpType
AX = mybir.AxisListType

NCORES = 8
B = 2
N = 16384
C = 1024
TLOC = N // NCORES
TILE = 128
NT = TLOC // TILE
TW = TLOC + TILE
SCALE2 = 1.0 / float(N)
INV_C = 1.0 / float(C)
RS_HALF = float(np.sqrt(0.5))


def _even(ap):
    return ap.rearrange("p (i two) -> p i two", two=2)[:, :, 0]


def _odd(ap):
    return ap.rearrange("p (i two) -> p i two", two=2)[:, :, 1]


def _ck(ap):
    return ap.rearrange("p (ck t) -> p ck t", ck=8)


def build_a(ntb=NT):
    nc = bass.Bass("TRN2", target_bir_lowering=False,
                   detect_race_conditions=False)
    xcm = nc.dram_tensor("xcm", [B, C, TW], F32, kind="ExternalInput").ap()
    tab = nc.dram_tensor("tab", [TW, C], F32, kind="ExternalInput").ap()
    wk = nc.dram_tensor("wk", [C, C], F32, kind="ExternalInput").ap()
    wv = nc.dram_tensor("wv", [C, C], F32, kind="ExternalInput").ap()
    bias3 = nc.dram_tensor("bias3", [3, C], F32, kind="ExternalInput").ap()
    kvp = nc.dram_tensor("kvp", [128, 2 * C], F32, kind="ExternalOutput").ap()
    ksp = nc.dram_tensor("ksp", [1, 2 * C], F32, kind="ExternalOutput").ap()

    ctx = ExitStack()
    sb = lambda nm, shp, dt=F32: ctx.enter_context(nc.sbuf_tensor(nm, shp, dt))
    ps = lambda nm, shp, dt=F32: ctx.enter_context(nc.psum_tensor(nm, shp, dt))
    sem = lambda nm: ctx.enter_context(nc.semaphore(nm))

    wk_sb = sb("wk_sb", [128, 8 * C], BF16)
    wv_sb = sb("wv_sb", [128, 8 * C], BF16)
    bk_sb = sb("bk_sb", [1, C], BF16)
    bv_sb = sb("bv_sb", [1, C], BF16)
    ones1 = sb("ones1", [1, 128], BF16)
    ones_f = sb("ones_f", [128, 1], F32)
    xbf = [sb(f"xbf{p}", [128, 8 * TILE], BF16) for p in range(2)]
    tabt = [sb(f"tabt{p}", [128, C], F32) for p in range(2)]
    k_sb = [sb(f"k_sb{p}", [128, C], F32) for p in range(2)]
    kr_bf = [sb(f"kr_bf{p}", [128, C], BF16) for p in range(2)]
    v_bf = [sb(f"v_bf{p}", [128, C], BF16) for p in range(2)]
    t1 = sb("t1", [128, 512], F32)
    t2 = sb("t2", [128, 512], F32)
    ksacc = [sb(f"ksacc{bb}", [128, C], F32) for bb in range(2)]
    kvs_sb = sb("kvs_sb", [128, 2 * C], F32)
    ksp_sb = sb("ksp_sb", [1, 2 * C], F32)

    PK = [ps("PK0", [128, C]), ps("PK1", [128, C])]
    PV = ps("PV", [128, C])
    PKV = ps("PKV", [128, C])

    s_xcE, s_xcO = sem("s_xcE"), sem("s_xcO")
    s_tabE, s_tabO = sem("s_tabE"), sem("s_tabO")
    s_w, s_gc = sem("s_w"), sem("s_gc")
    s_tk, s_tv, s_t2 = sem("s_tk"), sem("s_tv"), sem("s_t2")
    s_gel, s_vev = sem("s_gel"), sem("s_vev")
    s_v1, s_kve = sem("s_v1"), sem("s_kve")
    s_ksf, s_kse = sem("s_ksf"), sem("s_kse")
    s_out = sem("s_out")

    J = 2 * ntb

    xc_sem = lambda j: s_xcE if j % 2 == 0 else s_xcO
    tab_sem = lambda j: s_tabE if j % 2 == 0 else s_tabO

    with nc.Block() as block:
        @block.gpsimd
        def _(g):
            g.dma_start(out=wk_sb[:].rearrange("p (ck d) -> p ck d", ck=8), in_=wk.rearrange("(ck p) d -> p ck d", p=128)).then_inc(s_w, 16)
            g.dma_start(out=wv_sb[:].rearrange("p (ck d) -> p ck d", ck=8), in_=wv.rearrange("(ck p) d -> p ck d", p=128)).then_inc(s_w, 16)
            g.dma_start(out=bk_sb[:], in_=bias3[1:2, :]).then_inc(s_w, 16)
            g.dma_start(out=bv_sb[:], in_=bias3[2:3, :]).then_inc(s_w, 16)
            g.memset(ones1[:], 1.0)
            g.memset(ones_f[:], 1.0).then_inc(s_gc, 1)
            for j in range(J):
                b, i = divmod(j, ntb)
                if j >= 2:
                    g.wait_ge(s_tv, j - 1)
                row = (i + 1) * TILE
                src = xcm[b, :, row:row + TILE].rearrange(
                    "(ck p) t -> p ck t", p=128)
                g.dma_start(out=xbf[j % 2][:].rearrange("p (ck t) -> p ck t", ck=8),
                            in_=src).then_inc(xc_sem(j), 16)

        @block.sync
        def _(sy):
            for j in range(J):
                b, i = divmod(j, ntb)
                if j >= 2:
                    sy.wait_ge(s_v1, j - 1)
                row = (i + 1) * TILE
                sy.dma_start(out=tabt[j % 2][:], in_=tab[row:row + TILE, :]
                             ).then_inc(tab_sem(j), 16)
            sy.wait_ge(s_kve, 2)
            sy.wait_ge(s_kse, 2)
            sy.dma_start(out=kvp[:], in_=kvs_sb[:]).then_inc(s_out, 16)
            sy.dma_start(out=ksp[:], in_=ksp_sb[:]).then_inc(s_out, 16)
            sy.wait_ge(s_out, 32)

        @block.tensor
        def _(t):
            t.wait_ge(s_w, 64)
            t.wait_ge(s_gc, 1)

            def kv_mms(jj):
                bb, ii = divmod(jj, ntb)
                t.wait_ge(s_v1, jj + 1)
                t.wait_ge(s_vev, jj + 1)
                if ii == 0 and bb == 1:
                    t.wait_ge(s_kve, 1)
                for p in range(8):
                    st = (ii == 0) and (p in (0, 4))
                    sp = (ii == ntb - 1) and (p in (3, 7))
                    mm = t.matmul(PKV[:, p * 128:(p + 1) * 128],
                                  kr_bf[jj % 2][:, p * 128:(p + 1) * 128],
                                  v_bf[jj % 2][:, p * 128:(p + 1) * 128],
                                  start=st, stop=sp)
                mm.then_inc(s_t2, 1)

            for j in range(J):
                t.wait_ge(xc_sem(j), 16 * (j // 2 + 1))
                if j >= 2:
                    t.wait_ge(s_gel, j - 1)
                pk = PK[j % 2]
                xb = xbf[j % 2]
                for ck in range(8):
                    for h in range(2):
                        t.matmul(pk[:, h * 512:(h + 1) * 512],
                                 xb[:, ck * 128:(ck + 1) * 128],
                                 wk_sb[:, ck * C + h * 512: ck * C + (h + 1) * 512],
                                 start=(ck == 0), stop=False)
                for h in range(2):
                    mm = t.matmul(pk[:, h * 512:(h + 1) * 512], ones1[:],
                                  bk_sb[0:1, h * 512:(h + 1) * 512],
                                  start=False, stop=True)
                mm.then_inc(s_tk, 1)
                if j >= 1:
                    t.wait_ge(s_vev, j)
                for ck in range(8):
                    for h in range(2):
                        t.matmul(PV[:, h * 512:(h + 1) * 512],
                                 xb[:, ck * 128:(ck + 1) * 128],
                                 wv_sb[:, ck * C + h * 512: ck * C + (h + 1) * 512],
                                 start=(ck == 0), stop=False)
                for h in range(2):
                    mm = t.matmul(PV[:, h * 512:(h + 1) * 512], ones1[:],
                                  bv_sb[0:1, h * 512:(h + 1) * 512],
                                  start=False, stop=True)
                mm.then_inc(s_tv, 1)
                if j >= 1:
                    kv_mms(j - 1)
            kv_mms(J - 1)
            t.wait_ge(s_kve, 2)
            t.wait_ge(s_v1, J)
            for b in range(2):
                if b == 1:
                    t.wait_ge(s_kse, 1)
                for h in range(2):
                    mm = t.matmul(PKV[0:1, h * 512:(h + 1) * 512], ones_f[:],
                                  ksacc[b][:, h * 512:(h + 1) * 512],
                                  start=True, stop=True)
                mm.then_inc(s_ksf, 1)

        @block.scalar
        def _(s):
            for j in range(J):
                s.wait_ge(s_tk, j + 1)
                if j >= 2:
                    s.wait_ge(s_v1, j - 1)
                s.activation(k_sb[j % 2][:], PK[j % 2][:], AF.Gelu
                             ).then_inc(s_gel, 1)
                s.wait_ge(s_tv, j + 1)
                if j >= 2:
                    s.wait_ge(s_t2, j - 1)
                s.copy(v_bf[j % 2][:], PV[:]).then_inc(s_vev, 1)

        @block.vector
        def _(v):
            for j in range(J):
                b, i = divmod(j, ntb)
                v.wait_ge(s_gel, j + 1)
                v.wait_ge(tab_sem(j), 16 * (j // 2 + 1))
                if j >= 2:
                    v.wait_ge(s_t2, j - 1)
                ks = k_sb[j % 2]
                if i == 0:
                    v.tensor_copy(ksacc[b][:], ks[:])
                else:
                    v.tensor_add(ksacc[b][:], ksacc[b][:], ks[:])
                cosv = tabt[j % 2][:, 0:512]
                sinv = tabt[j % 2][:, 512:1024]
                kr = kr_bf[j % 2]
                v.scalar_tensor_tensor(t1[:], _even(ks[:]), 0.21, cosv,
                                       ALU.add, ALU.mult)
                v.scalar_tensor_tensor(t2[:], _odd(ks[:]), 0.21, sinv,
                                       ALU.add, ALU.mult)
                v.drain()
                v.tensor_sub(_even(kr[:]), t1[:], t2[:])
                v.drain()
                v.scalar_tensor_tensor(t1[:], _odd(ks[:]), 0.21, cosv,
                                       ALU.add, ALU.mult)
                v.scalar_tensor_tensor(t2[:], _even(ks[:]), 0.21, sinv,
                                       ALU.add, ALU.mult)
                v.drain()
                v.tensor_add(_odd(kr[:]), t1[:], t2[:]).then_inc(s_v1, 1)
                if i == ntb - 1:
                    v.wait_ge(s_t2, 16 * (b + 1))
                    v.tensor_scalar(kvs_sb[:, b * C:(b + 1) * C], PKV[:],
                                    SCALE2, None, ALU.mult).then_inc(s_kve, 1)
            for b in range(2):
                v.wait_ge(s_ksf, b + 1)
                v.tensor_copy(ksp_sb[0:1, b * C:(b + 1) * C], PKV[0:1, :]).then_inc(s_kse, 1)

    ctx.close()
    return nc


def build_b(tj=None, debug=False):
    nc = bass.Bass("TRN2", target_bir_lowering=False,
                   detect_race_conditions=False)
    xcm = nc.dram_tensor("xcm", [B, C, TW], F32, kind="ExternalInput").ap()
    xtm = nc.dram_tensor("xtm", [B, TW, C], F32, kind="ExternalInput").ap()
    tab = nc.dram_tensor("tab", [TW, C], F32, kind="ExternalInput").ap()
    wq = nc.dram_tensor("wq", [C, C], F32, kind="ExternalInput").ap()
    wr = nc.dram_tensor("wr", [C, C], F32, kind="ExternalInput").ap()
    wkf = nc.dram_tensor("wkf", [C, C], F32, kind="ExternalInput").ap()
    wvf = nc.dram_tensor("wvf", [C, C], F32, kind="ExternalInput").ap()
    bias3 = nc.dram_tensor("bias3", [3, C], F32, kind="ExternalInput").ap()
    gb = nc.dram_tensor("gb", [2, C], F32, kind="ExternalInput").ap()
    mu2 = nc.dram_tensor("mu2", [128, 16], F32, kind="ExternalInput").ap()
    mask = nc.dram_tensor("mask", [128, 1], F32, kind="ExternalInput").ap()
    ident = nc.dram_tensor("ident", [128, 128], F32, kind="ExternalInput").ap()
    kv_all = nc.dram_tensor("kv_all", [NCORES, 128, 2 * C], F32,
                            kind="ExternalInput").ap()
    ks_in = nc.dram_tensor("ks_in", [NCORES, 2 * C], F32,
                           kind="ExternalInput").ap()
    y = nc.dram_tensor("y", [B, TLOC, C], F32, kind="ExternalOutput").ap()
    dbg = (nc.dram_tensor("dbg", [8, 128, C], F32, kind="ExternalOutput").ap()
           if debug else None)
    dbgs = (nc.dram_tensor("dbgs", [128, 16 + 16 + 8 + 1040], F32,
                           kind="ExternalOutput").ap() if debug else None)
    km_dr = nc.dram_tensor("km_dr", [1, 2 * C], F32, kind="Internal").ap()
    ch_dr = nc.dram_tensor("ch_dr", [1, 32], F32, kind="Internal").ap()

    ctx = ExitStack()
    sb = lambda nm, shp, dt=F32: ctx.enter_context(nc.sbuf_tensor(nm, shp, dt))
    ps = lambda nm, shp, dt=F32: ctx.enter_context(nc.psum_tensor(nm, shp, dt))
    sem = lambda nm: ctx.enter_context(nc.semaphore(nm))

    wq_sb = sb("wq_sb", [128, 8 * C], BF16)
    wr_sb = sb("wr_sb", [128, 8 * C], BF16)
    wkf_sb = sb("wkf_sb", [128, 8 * C], BF16)
    wvf_sb = sb("wvf_sb", [128, 8 * C], BF16)
    bq_sb = sb("bq_sb", [1, C], BF16)
    ones1 = sb("ones1", [1, 128], BF16)
    identb = sb("identb", [128, 128], BF16)
    identf = sb("identf", [128, 128], F32)
    mu_sb = sb("mu_sb", [128, 16], F32)
    mask_sb = sb("mask_sb", [128, 1], F32)
    kvacc = sb("kvacc", [128, 2 * C], F32)
    kv_bd = sb("kv_bd", [128, 2 * C], BF16)
    kmb = sb("kmb", [128, 2 * C], F32)
    chb = sb("chb", [128, 32], F32)
    gamb = sb("gamb", [128, C], F32)
    betb = sb("betb", [128, C], F32)

    xbf = [sb(f"xbf{p}", [128, 8 * TILE], BF16) for p in range(2)]
    tabt = [sb(f"tabt{p}", [128, C], F32) for p in range(2)]
    xts = [sb(f"xts{p}", [128, C], F32) for p in range(2)]
    q_sb = sb("q_sb", [128, C], F32)
    zt_sb = sb("zt_sb", [128, C], F32)
    qz_sb = zt_sb
    z16 = sb("z16", [128, 16], F32)
    qr_bf = [sb(f"qr_bf{p}", [128, C], BF16) for p in range(2)]
    qrT_sb = [sb(f"qrT_sb{p}", [128, C], BF16) for p in range(2)]
    t1 = sb("t1", [128, 512], F32)
    t2 = sb("t2", [128, 512], F32)
    res1 = sb("res1", [128, C], F32)
    y1_sb = sb("y1_sb", [128, C], F32)
    xb_sb = sb("xb_sb", [128, C], F32)
    u_sb = xb_sb
    tt_sb = sb("tt_sb", [128, C], F32)
    fo_sb = tt_sb
    res2 = sb("res2", [128, C], F32)
    x1_tm = [sb(f"x1_tm{p}", [128, C], F32) for p in range(2)]
    ring = [sb("ring0", [128, 8 * 130], F32), sb("ring1", [128, 8 * 130], F32)]
    d_sb = sb("d_sb", [128, C], F32)
    u2_sb = sb("u2_sb", [128, C], F32)
    xr_bf = sb("xr_bf", [128, C], BF16)
    xk_bf = sb("xk_bf", [128, C], BF16)
    tr_sb = sb("tr_sb", [128, C], F32)
    kact_bf = sb("kact_bf", [128, C], BF16)
    kactT_sb = sb("kactT_sb", [128, C], BF16)
    res3 = sb("res3", [128, C], F32)
    y_sb = [kvacc[:, 0:C], kvacc[:, C:2 * C]]
    dump = q_sb
    krelu = q_sb
    sumst = sb("sumst", [128, 4], F32)
    sqst = sb("sqst", [128, 4], F32)
    stt_ = sb("stt_", [128, 16], F32)
    st2_ = sb("st2_", [128, 2], F32)
    eps12 = sb("eps12", [128, 1], F32)
    eps5 = sb("eps5", [128, 1], F32)

    PQ = ps("PQ", [128, C])
    PTQ = ps("PTQ", [128, C], BF16)
    PTK = ps("PTK", [128, C], BF16)
    PO = ps("PO", [128, C])
    PX = ps("PX", [128, C])

    names = ["sVZ", "sTQ", "sSG", "sTT", "sEQ", "sTO", "sR1", "sS1", "sVa1",
             "sSb1", "sVc1", "sS2", "sR2", "sS3", "sVa2", "sSb2", "sVc2",
             "sS4", "sTX", "sVR", "sVM", "sGM", "sTR", "sSR", "sTK", "sSKr",
             "sSK", "sTKT", "sEK", "sTV", "sVG", "sS5", "sVa3", "sSb3",
             "sVc3", "sS6"]
    S = {nm: sem(nm) for nm in names}
    s_xcE, s_xcO = sem("s_xcE"), sem("s_xcO")
    s_tabE, s_tabO = sem("s_tabE"), sem("s_tabO")
    s_xtE, s_xtO = sem("s_xtE"), sem("s_xtO")
    s_w = sem("s_w")
    s_w2 = sem("s_w2")
    s_acc = sem("s_acc")
    s_ack = sem("s_ack")
    s_set = sem("s_set")
    s_kmb = sem("s_kmb")
    s_gc = sem("s_gc")
    s_y = sem("s_y")
    s_vd = sem("s_vd")
    s_sd = sem("s_sd")
    s_gd = sem("s_gd")
    sVC = sem("sVC")
    sGX = sem("sGX")

    def stc(ln, k):
        return stt_[:, ln * 5 + k: ln * 5 + k + 1]

    mur_view = mu_sb[:, 0:8].unsqueeze(2).broadcast_to([128, 8, 128])
    muk_view = mu_sb[:, 8:16].unsqueeze(2).broadcast_to([128, 8, 128])
    ring_v = [ring[b][:].rearrange("p (ck w) -> p ck w", w=130) for b in range(2)]
    x_v = [ring_v[b][:, :, 1:129] for b in range(2)]
    xp_v = [ring_v[b][:, :, 0:128] for b in range(2)]
    colL = [ring_v[b][:, :, 128:129] for b in range(2)]
    col0 = [ring_v[b][:, :, 0:1] for b in range(2)]

    xc_sem = lambda j: s_xcE if j % 2 == 0 else s_xcO
    tab_sem = lambda j: s_tabE if j % 2 == 0 else s_tabO
    xt_sem = lambda j: s_xtE if j % 2 == 0 else s_xtO

    tiles = [(b, i) for i in range(NT + 1) for b in range(2)]
    if tj is not None:
        tiles = tiles[:tj]
    real_ord = {}
    ro = 0
    for j, (b, i) in enumerate(tiles):
        if i > 0:
            real_ord[j] = ro
            ro += 1
    NREAL = ro
    J = len(tiles)

    with nc.Block() as block:
        @block.gpsimd
        def _(g):
            gdc = [0]

            def gsync():
                g.drain()

            g.dma_start(out=wq_sb[:].rearrange("p (ck d) -> p ck d", ck=8), in_=wq.rearrange("(ck p) d -> p ck d", p=128)).then_inc(s_w, 16)
            g.dma_start(out=wr_sb[:].rearrange("p (ck d) -> p ck d", ck=8), in_=wr.rearrange("(ck p) d -> p ck d", p=128)).then_inc(s_w, 16)
            g.dma_start(out=wkf_sb[:].rearrange("p (ck d) -> p ck d", ck=8), in_=wkf.rearrange("(ck p) d -> p ck d", p=128)).then_inc(s_w, 16)
            g.dma_start(out=wvf_sb[:].rearrange("p (ck d) -> p ck d", ck=8), in_=wvf.rearrange("(ck p) d -> p ck d", p=128)).then_inc(s_w, 16)
            g.dma_start(out=bq_sb[:], in_=bias3[0:1, :]).then_inc(s_w, 16)
            g.dma_start(out=identb[:], in_=ident[:]).then_inc(s_w, 16)
            g.dma_start(out=gamb[:], in_=gb[0:1, :].broadcast_to([128, C])).then_inc(s_w, 16)
            g.dma_start(out=betb[:], in_=gb[1:2, :].broadcast_to([128, C])).then_inc(s_w, 16)
            g.memset(ones1[:], 1.0)
            g.memset(eps12[:], 1e-12)
            g.memset(eps5[:], 1e-5).then_inc(s_gc, 1)
            for j in (0, 1)[:len(tiles)]:
                b, i = tiles[j]
                src = xcm[b, :, i * TILE:(i + 1) * TILE].rearrange(
                    "(ck p) t -> p ck t", p=128)
                g.dma_start(out=xbf[j % 2][:].rearrange("p (ck t) -> p ck t", ck=8),
                            in_=src).then_inc(xc_sem(j), 16)
            g.dma_start(out=kvacc[:], in_=kv_all[0]).then_inc(s_acc, 16)
            for k in range(1, NCORES):
                g.wait_ge(s_acc, 16 * k)
                g.dma_start(out=kvacc[:], in_=kv_all[k], accum_op=ALU.add
                            ).then_inc(s_acc, 16)
            g.dma_start(out=kmb[0:1, :], in_=ks_in[0:1, :]).then_inc(s_ack, 16)
            for k in range(1, NCORES):
                g.wait_ge(s_ack, 16 * k)
                g.dma_start(out=kmb[0:1, :], in_=ks_in[k:k + 1, :],
                            accum_op=ALU.add).then_inc(s_ack, 16)
            g.wait_ge(s_set, 1)
            g.dma_start(out=km_dr[:], in_=kmb[0:1, :]).then_inc(s_kmb, 16)
            g.dma_start(out=ch_dr[:], in_=chb[0:1, 0:32]).then_inc(s_kmb, 16)
            g.wait_ge(s_kmb, 32)
            g.dma_start(out=kmb[:, 0:C], in_=km_dr[0:1, 0:C].broadcast_to([128, C])).then_inc(s_kmb, 16)
            g.dma_start(out=kmb[:, C:2 * C], in_=km_dr[0:1, C:2 * C].broadcast_to([128, C])).then_inc(s_kmb, 16)
            g.dma_start(out=chb[:, 0:16], in_=ch_dr[0:1, 0:16].broadcast_to([128, 16])).then_inc(s_kmb, 16)
            g.dma_start(out=chb[:, 16:32], in_=ch_dr[0:1, 16:32].broadcast_to([128, 16])).then_inc(s_kmb, 16)
            for j, (b, i) in enumerate(tiles):
                if j >= 2:
                    g.wait_ge(S["sTQ"], j - 1)
                    src = xcm[b, :, i * TILE:(i + 1) * TILE].rearrange(
                        "(ck p) t -> p ck t", p=128)
                    g.dma_start(out=xbf[j % 2][:].rearrange("p (ck t) -> p ck t", ck=8),
                                in_=src).then_inc(xc_sem(j), 16)
                if j == 0:
                    g.wait_ge(s_w, 16 * 8)
                g.wait_ge(xt_sem(j), 16 * (j // 2 + 1))
                if j >= 1:
                    g.wait_ge(S["sR2"], j)
                g.tensor_add(u2_sb[:], xts[j % 2][:], betb[:]).then_inc(sGX, 1)
                if i > 0:
                    g.wait_ge(S["sVM"], j + 1)
                    g.wait_ge(S["sTK"], j)
                    g.tensor_tensor(_ck(d_sb[:]), _ck(d_sb[:]), muk_view,
                                    ALU.mult)
                    gsync()
                    g.tensor_tensor(_ck(xk_bf[:]), xp_v[b], _ck(d_sb[:]),
                                    ALU.add).then_inc(S["sGM"], 1)
                else:
                    g.nop().then_inc(S["sGM"], 1)
            if debug:
                jl = len(tiles)
                for nm in ("sVR", "sS4", "sEQ", "sVZ", "sR2", "sVc1", "sVc2",
                           "sSG", "sS2", "sTO"):
                    g.wait_ge(S[nm], jl)
                g.dma_start(out=dbg[0], in_=qz_sb[:]).then_inc(s_kmb, 16)
                g.dma_start(out=dbg[1], in_=qr_bf[(jl - 1) % 2][:]).then_inc(s_kmb, 16)
                g.dma_start(out=dbg[2], in_=qrT_sb[(jl - 1) % 2][:]).then_inc(s_kmb, 16)
                g.dma_start(out=dbg[3], in_=res1[:]).then_inc(s_kmb, 16)
                g.dma_start(out=dbg[4], in_=y1_sb[:]).then_inc(s_kmb, 16)
                g.dma_start(out=dbg[5], in_=res2[:]).then_inc(s_kmb, 16)
                g.dma_start(out=dbg[6], in_=x1_tm[(jl - 1) % 2][:]).then_inc(s_kmb, 16)
                g.dma_start(out=dbg[7], in_=tt_sb[:]).then_inc(s_kmb, 16)
                g.dma_start(out=dbgs[:, 0:16], in_=z16[:]).then_inc(s_kmb, 16)
                g.dma_start(out=dbgs[:, 16:32], in_=stt_[:]).then_inc(s_kmb, 16)
                g.dma_start(out=dbgs[:, 32:36], in_=sumst[:]).then_inc(s_kmb, 16)
                g.dma_start(out=dbgs[:, 36:40], in_=sqst[:]).then_inc(s_kmb, 16)
                g.dma_start(out=dbgs[:, 40:1080], in_=ring[tiles[-1][0]][:]).then_inc(s_kmb, 16)

        @block.sync
        def _(sy):
            sy.dma_start(out=identf[:], in_=ident[:]).then_inc(s_w2, 16)
            sy.dma_start(out=mu_sb[:], in_=mu2[:]).then_inc(s_w2, 16)
            sy.dma_start(out=mask_sb[:], in_=mask[:]).then_inc(s_w2, 16)
            for j, (b, i) in enumerate(tiles):
                row = i * TILE
                if j >= 2:
                    sy.wait_ge(S["sVZ"], j - 1)
                sy.dma_start(out=tabt[j % 2][:], in_=tab[row:row + TILE, :]
                             ).then_inc(tab_sem(j), 16)
                if j >= 2:
                    sy.wait_ge(S["sR2"], j - 1)
                    sy.wait_ge(sGX, j - 1)
                sy.dma_start(out=xts[j % 2][:], in_=xtm[b, row:row + TILE, :]
                             ).then_inc(xt_sem(j), 16)
                if i > 0:
                    sy.wait_ge(S["sS6"], j + 1)
                    sy.dma_start(out=y[b, (i - 1) * TILE:i * TILE, :],
                                 in_=y_sb[j % 2]).then_inc(s_y, 16)
            if NREAL > 0:
                sy.wait_ge(s_y, 16 * NREAL)
            if debug:
                sy.wait_ge(s_kmb, 16 * 19)

        @block.tensor
        def _(t):
            t.wait_ge(s_w, 16 * 8)
            t.wait_ge(s_gc, 1)
            t.wait_ge(s_w2, 16 * 3)
            for j, (b, i) in enumerate(tiles):
                real = i > 0
                t.wait_ge(xc_sem(j), 16 * (j // 2 + 1))
                if j >= 1:
                    t.wait_ge(S["sSR"], j)
                xbfj = xbf[j % 2]
                for ck in range(8):
                    for h in range(2):
                        t.matmul(PQ[:, h * 512:(h + 1) * 512],
                                 xbfj[:, ck * 128:(ck + 1) * 128],
                                 wq_sb[:, ck * C + h * 512: ck * C + (h + 1) * 512],
                                 start=(ck == 0), stop=False)
                for h in range(2):
                    mm = t.matmul(PQ[:, h * 512:(h + 1) * 512], ones1[:],
                                  bq_sb[0:1, h * 512:(h + 1) * 512],
                                  start=False, stop=True)
                mm.then_inc(S["sTQ"], 1)
                t.wait_ge(S["sVZ"], j + 1)
                if j >= 1:
                    t.wait_ge(S["sEQ"], j)
                for ck in range(8):
                    mm = t.transpose(PTQ[:, ck * 128:(ck + 1) * 128],
                                     qr_bf[j % 2][:, ck * 128:(ck + 1) * 128],
                                     identb[:])
                mm.then_inc(S["sTT"], 1)
                t.wait_ge(S["sEQ"], j + 1)
                if j >= 1:
                    t.wait_ge(S["sSKr"], j)
                if j == 0:
                    t.wait_ge(s_set, 2)
                for p in range(8):
                    mm = t.matmul(PO[:, p * 128:(p + 1) * 128],
                                  qrT_sb[j % 2][:, p * 128:(p + 1) * 128],
                                  kv_bd[:, b * C + p * 128: b * C + (p + 1) * 128],
                                  start=(p in (0, 4)), stop=(p in (3, 7)))
                mm.then_inc(S["sTO"], 1)
                t.wait_ge(S["sS4"], j + 1)
                if j >= 1:
                    t.wait_ge(S["sVG"], j)
                for ck in range(8):
                    mm = t.transpose(PX[:, ck * 128:(ck + 1) * 128],
                                     x1_tm[j % 2][:, ck * 128:(ck + 1) * 128],
                                     identf[:])
                mm.then_inc(S["sTX"], 1)
                if real:
                    t.wait_ge(S["sVM"], j + 1)
                    t.wait_ge(S["sSG"], j + 1)
                    for ck in range(8):
                        for h in range(2):
                            mm = t.matmul(PQ[:, h * 512:(h + 1) * 512],
                                          xr_bf[:, ck * 128:(ck + 1) * 128],
                                          wr_sb[:, ck * C + h * 512: ck * C + (h + 1) * 512],
                                          start=(ck == 0), stop=(ck == 7))
                    mm.then_inc(S["sTR"], 1)
                    t.wait_ge(S["sGM"], j + 1)
                    t.wait_ge(S["sR1"], j + 1)
                    for ck in range(8):
                        for h in range(2):
                            mm = t.matmul(PO[:, h * 512:(h + 1) * 512],
                                          xk_bf[:, ck * 128:(ck + 1) * 128],
                                          wkf_sb[:, ck * C + h * 512: ck * C + (h + 1) * 512],
                                          start=(ck == 0), stop=(ck == 7))
                    mm.then_inc(S["sTK"], 1)
                    t.wait_ge(S["sSK"], j + 1)
                    if j >= 1:
                        t.wait_ge(S["sEK"], j)
                    for ck in range(8):
                        mm = t.transpose(PTK[:, ck * 128:(ck + 1) * 128],
                                         kact_bf[:, ck * 128:(ck + 1) * 128],
                                         identb[:])
                    mm.then_inc(S["sTKT"], 1)
                    t.wait_ge(S["sEK"], j + 1)
                    t.wait_ge(S["sVR"], j + 1)
                    for ck in range(8):
                        for h in range(2):
                            mm = t.matmul(PX[:, h * 512:(h + 1) * 512],
                                          kactT_sb[:, ck * 128:(ck + 1) * 128],
                                          wvf_sb[:, ck * C + h * 512: ck * C + (h + 1) * 512],
                                          start=(ck == 0), stop=(ck == 7))
                    mm.then_inc(S["sTV"], 1)
                else:
                    if j >= 1:
                        t.wait_ge(S["sEK"], j)
                    for nm in ("sTR", "sTK", "sTKT", "sTV"):
                        t.transpose(PTK[:, 0:128], identb[:], identb[:]
                                    ).then_inc(S[nm], 1)

        @block.scalar
        def _(s):
            sdc = [0]

            def ssync():
                s.drain()

            s.wait_ge(s_gc, 1)
            for j, (b, i) in enumerate(tiles):
                real = i > 0
                s.wait_ge(S["sTQ"], j + 1)
                if j >= 1:
                    s.wait_ge(S["sVZ"], j)
                s.activation(q_sb[:], PQ[:], AF.Gelu).then_inc(S["sSG"], 1)
                s.wait_ge(S["sTT"], j + 1)
                if j >= 2:
                    s.wait_ge(S["sTO"], j - 1)
                s.copy(qrT_sb[j % 2][:], PTQ[:]).then_inc(S["sEQ"], 1)
                s.wait_ge(S["sR1"], j + 1)
                s.activation(dump[:], res1[:], AF.Square,
                             accum_out=sqst[:, 0:1])
                s.activation(stc(0, 0), sumst[:, 0:1], AF.Square, scale=INV_C
                             ).then_inc(S["sS1"], 1)
                s.wait_ge(S["sVa1"], j + 1)
                s.activation(stc(0, 2), stc(0, 1), AF.Sqrt,
                             bias=eps12[:, 0:1]).then_inc(S["sSb1"], 1)
                s.wait_ge(S["sVc1"], j + 1)
                if j >= 1:
                    s.wait_ge(S["sR2"], j)
                s.activation(y1_sb[:], res1[:], AF.Identity, bias=stc(0, 4),
                             scale=stc(0, 3)).then_inc(S["sS2"], 1)
                s.wait_ge(S["sR2"], j + 1)
                s.activation(dump[:], res2[:], AF.Square,
                             accum_out=sqst[:, 1:2])
                s.activation(stc(1, 0), sumst[:, 1:2], AF.Square, scale=INV_C
                             ).then_inc(S["sS3"], 1)
                s.wait_ge(S["sVa2"], j + 1)
                s.activation(stc(1, 2), stc(1, 1), AF.Sqrt,
                             bias=eps5[:, 0:1]).then_inc(S["sSb2"], 1)
                s.wait_ge(S["sVc2"], j + 1)
                if j >= 1:
                    s.wait_ge(S["sVG"], j)
                s.activation(x1_tm[j % 2][:], res2[:], AF.Identity, bias=stc(1, 4),
                             scale=stc(1, 3)).then_inc(S["sS4"], 1)
                s.wait_ge(sVC, j + 1)
                s.wait_ge(S["sTX"], j + 1)
                s.wait_ge(S["sGM"], j)
                if real:
                    s.copy(x_v[b], _ck(PX[:])).then_inc(S["sVR"], 1)
                else:
                    s.copy(x_v[b], _ck(PX[:]))
                    ssync()
                    s.activation(colL[b].squeeze(2), colL[b].squeeze(2),
                                 AF.Identity,
                                 scale=mask_sb[:, 0:1]).then_inc(S["sVR"], 1)
                if real:
                    s.wait_ge(S["sTR"], j + 1)
                    s.wait_ge(S["sVG"], j)
                    s.activation(tr_sb[:], PQ[:], AF.Tanh, scale=0.5
                                 ).then_inc(S["sSR"], 1)
                    s.wait_ge(S["sTK"], j + 1)
                    s.activation(krelu[:], PO[:], AF.Relu).then_inc(S["sSKr"], 1)
                    ssync()
                    s.wait_ge(S["sTKT"], j)
                    s.activation(kact_bf[:], krelu[:], AF.Square, scale=RS_HALF
                                 ).then_inc(S["sSK"], 1)
                    s.wait_ge(S["sTKT"], j + 1)
                    s.wait_ge(S["sTV"], j)
                    s.copy(kactT_sb[:], PTK[:]).then_inc(S["sEK"], 1)
                    s.wait_ge(S["sVG"], j + 1)
                    s.activation(dump[:], res3[:], AF.Square,
                                 accum_out=sqst[:, 2:3])
                    s.activation(stc(2, 0), sumst[:, 2:3], AF.Square, scale=INV_C
                                 ).then_inc(S["sS5"], 1)
                    s.wait_ge(S["sVa3"], j + 1)
                    s.activation(stc(2, 2), stc(2, 1), AF.Sqrt,
                                 bias=eps5[:, 0:1]).then_inc(S["sSb3"], 1)
                    s.wait_ge(S["sVc3"], j + 1)
                    ord_ = real_ord[j]
                    if ord_ >= 2:
                        s.wait_ge(s_y, 16 * (ord_ - 1))
                    s.activation(y_sb[j % 2], res3[:], AF.Identity,
                                 bias=stc(2, 4), scale=stc(2, 3)
                                 ).then_inc(S["sS6"], 1)
                else:
                    for nm in ("sSR", "sSKr", "sSK", "sEK", "sS5", "sSb3",
                               "sS6"):
                        s.nop().then_inc(S[nm], 1)

        @block.vector
        def _(v):
            vdc = [0]

            def vsync():
                v.drain()

            v.memset(ring[0][:], 0.0)
            v.memset(ring[1][:], 0.0)
            if debug:
                v.memset(stt_[:], 0.0)
                v.memset(sumst[:], 0.0)
                v.memset(sqst[:], 0.0)
            v.wait_ge(s_acc, 16 * NCORES)
            v.wait_ge(s_ack, 16 * NCORES)
            v.wait_ge(s_w2, 16 * 3)
            v.tensor_scalar(kmb[0:1, :], kmb[0:1, :], 1.0 / float(N), 0.21,
                            ALU.mult, ALU.add)
            vsync()
            v.reduce_sum(chb[0:1, 0:32],
                         kmb[0:1, :].rearrange("p (h e) -> p h e", h=32),
                         axis=AX.X)
            vsync()
            v.tensor_scalar(chb[0:1, 0:32], chb[0:1, 0:32], 0.21, None,
                            ALU.mult).then_inc(s_set, 1)
            v.memset(kv_bd[:], 0.0)
            vsync()
            for bb in range(2):
                for p in range(8):
                    off = bb * C + p * 128
                    v.tensor_copy(kv_bd[0:64, off:off + 64],
                                  kvacc[0:64, off:off + 64])
                    v.tensor_copy(kv_bd[64:128, off + 64:off + 128],
                                  kvacc[64:128, off + 64:off + 128])
            v.nop().then_inc(s_set, 1)
            for j, (b, i) in enumerate(tiles):
                real = i > 0
                v.wait_ge(S["sSG"], j + 1)
                v.wait_ge(tab_sem(j), 16 * (j // 2 + 1))
                if j == 0:
                    v.wait_ge(s_kmb, 16 * 6)
                v.tensor_mul(zt_sb[:], q_sb[:], kmb[:, b * C:(b + 1) * C])
                v.reduce_sum(z16[:], zt_sb[:].rearrange("p (h e) -> p h e", h=16),
                             axis=AX.X)
                vsync()
                v.tensor_add(z16[:], z16[:], chb[:, b * 16:(b + 1) * 16])
                vsync()
                v.tensor_scalar(z16[:], z16[:], 1e-6, None, ALU.add)
                vsync()
                v.reciprocal(z16[:], z16[:])
                vsync()
                zview = z16[:].unsqueeze(2).broadcast_to([128, 16, 64])
                v.scalar_tensor_tensor(
                    qz_sb[:].rearrange("p (h e) -> p h e", h=16),
                    q_sb[:].rearrange("p (h e) -> p h e", h=16),
                    0.21, zview, ALU.add, ALU.mult)
                cosv = tabt[j % 2][:, 0:512]
                sinv = tabt[j % 2][:, 512:1024]
                if j >= 2:
                    v.wait_ge(S["sTT"], j - 1)
                qr = qr_bf[j % 2]
                v.tensor_mul(t1[:], _even(qz_sb[:]), cosv)
                v.tensor_mul(t2[:], _odd(qz_sb[:]), sinv)
                v.tensor_sub(_even(qr[:]), t1[:], t2[:])
                v.tensor_mul(t1[:], _odd(qz_sb[:]), cosv)
                v.tensor_mul(t2[:], _even(qz_sb[:]), sinv)
                v.tensor_add(_odd(qr[:]), t1[:], t2[:]).then_inc(S["sVZ"], 1)
                v.wait_ge(S["sTO"], j + 1)
                v.wait_ge(xt_sem(j), 16 * (j // 2 + 1))
                if j >= 1:
                    v.wait_ge(S["sS2"], j)
                v.scalar_tensor_tensor(res1[:], PO[:], 1.0, xts[j % 2][:],
                                       ALU.mult, ALU.add,
                                       accum_out=sumst[:, 0:1]).then_inc(S["sR1"], 1)
                v.wait_ge(S["sS1"], j + 1)
                v.tensor_scalar(stc(0, 1), sqst[:, 0:1], INV_C, stc(0, 0),
                                ALU.mult, ALU.subtract).then_inc(S["sVa1"], 1)
                v.wait_ge(S["sSb1"], j + 1)
                v.reciprocal(stc(0, 3), stc(0, 2))
                vsync()
                v.tensor_scalar(stc(0, 4), sumst[:, 0:1], stc(0, 3), -INV_C,
                                ALU.mult, ALU.mult).then_inc(S["sVc1"], 1)
                v.wait_ge(S["sS2"], j + 1)
                v.wait_ge(sGX, j + 1)
                if j >= 1:
                    v.wait_ge(S["sS4"], j)
                v.tensor_mul(tt_sb[:], y1_sb[:], gamb[:])
                v.scalar_tensor_tensor(res2[:], tt_sb[:], 1.0, u2_sb[:],
                                       ALU.mult, ALU.add,
                                       accum_out=sumst[:, 1:2]).then_inc(S["sR2"], 1)
                v.wait_ge(S["sS3"], j + 1)
                v.tensor_scalar(stc(1, 1), sqst[:, 1:2], INV_C, stc(1, 0),
                                ALU.mult, ALU.subtract).then_inc(S["sVa2"], 1)
                v.wait_ge(S["sSb2"], j + 1)
                v.reciprocal(stc(1, 3), stc(1, 2))
                vsync()
                v.tensor_scalar(stc(1, 4), sumst[:, 1:2], stc(1, 3), -INV_C,
                                ALU.mult, ALU.mult).then_inc(S["sVc2"], 1)
                if j >= 1:
                    v.wait_ge(S["sVR"], j)
                    v.wait_ge(S["sGM"], j)
                v.tensor_copy(col0[b], colL[b]).then_inc(sVC, 1)
                if real:
                    v.wait_ge(S["sVR"], j + 1)
                    v.wait_ge(S["sTR"], j)
                    v.tensor_sub(_ck(d_sb[:]), x_v[b], xp_v[b])
                    v.tensor_mul(_ck(u_sb[:]), _ck(d_sb[:]), mur_view)
                    v.tensor_add(_ck(xr_bf[:]), xp_v[b], _ck(u_sb[:])
                                 ).then_inc(S["sVM"], 1)
                    v.wait_ge(S["sTV"], j + 1)
                    v.wait_ge(S["sSR"], j + 1)
                    if j >= 1:
                        v.wait_ge(S["sS6"], j)
                    v.scalar_tensor_tensor(fo_sb[:], tr_sb[:], 1.0, PX[:],
                                           ALU.add, ALU.mult)
                    v.scalar_tensor_tensor(res3[:], fo_sb[:], 1.0,
                                           x1_tm[j % 2][:], ALU.mult, ALU.add,
                                           accum_out=sumst[:, 2:3]
                                           ).then_inc(S["sVG"], 1)
                    v.wait_ge(S["sS5"], j + 1)
                    v.tensor_scalar(stc(2, 1), sqst[:, 2:3], INV_C, stc(2, 0),
                                    ALU.mult, ALU.subtract).then_inc(S["sVa3"], 1)
                    v.wait_ge(S["sSb3"], j + 1)
                    v.reciprocal(stc(2, 3), stc(2, 2))
                    vsync()
                    v.tensor_scalar(stc(2, 4), sumst[:, 2:3], stc(2, 3), -INV_C,
                                    ALU.mult, ALU.mult).then_inc(S["sVc3"], 1)
                else:
                    for nm in ("sVM", "sVG", "sVa3", "sVc3"):
                        v.nop().then_inc(S[nm], 1)

    ctx.close()
    return nc


def kernel(L0_attn_x, L0_attn_Wq, L0_attn_bq, L0_attn_Wk, L0_attn_bk,
           L0_attn_Wv, L0_attn_bv, L0_attn_gamma, L0_attn_beta,
           L0_ffn_W_r, L0_ffn_W_k, L0_ffn_W_v, L0_ffn_mu_r, L0_ffn_mu_k):
    x = np.ascontiguousarray(np.asarray(L0_attn_x, dtype=np.float32))
    wqT = np.ascontiguousarray(np.asarray(L0_attn_Wq, np.float32).T)
    wkT = np.ascontiguousarray(np.asarray(L0_attn_Wk, np.float32).T)
    wvT = np.ascontiguousarray(np.asarray(L0_attn_Wv, np.float32).T)
    wrT = np.ascontiguousarray(np.asarray(L0_ffn_W_r, np.float32).T)
    wkfT = np.ascontiguousarray(np.asarray(L0_ffn_W_k, np.float32).T)
    wvfT = np.ascontiguousarray(np.asarray(L0_ffn_W_v, np.float32).T)
    bias3 = np.ascontiguousarray(np.stack([
        np.asarray(L0_attn_bq, np.float32),
        np.asarray(L0_attn_bk, np.float32),
        np.asarray(L0_attn_bv, np.float32)]))
    gbv = np.ascontiguousarray(np.stack([
        np.asarray(L0_attn_gamma, np.float32),
        np.asarray(L0_attn_beta, np.float32)]))
    mu2 = np.ascontiguousarray(np.concatenate([
        np.asarray(L0_ffn_mu_r, np.float32).reshape(8, 128).T,
        np.asarray(L0_ffn_mu_k, np.float32).reshape(8, 128).T], axis=1))
    ident = np.eye(128, dtype=np.float32)

    import jax
    import jax.numpy as jnp
    with jax.default_device(jax.devices("cpu")[0]):
        ivj = 1.0 / jnp.float32(10000.0) ** (
            jnp.arange(0, C, 2, dtype=jnp.float32) / C)
        angj = jnp.arange(N, dtype=jnp.float32)[:, None] * ivj[None, :]
        tab_full = np.ascontiguousarray(np.concatenate(
            [np.asarray(jnp.cos(angj)), np.asarray(jnp.sin(angj))],
            axis=1).astype(np.float32))

    xcm, xtm, tabs, masks = [], [], [], []
    for c in range(NCORES):
        s = c * TLOC
        if c == 0:
            xs = np.concatenate([x[:, 0:TILE, :], x[:, 0:TLOC, :]], axis=1)
            tb = np.concatenate([tab_full[0:TILE], tab_full[0:TLOC]], axis=0)
        else:
            xs = x[:, s - TILE:s + TLOC, :]
            tb = tab_full[s - TILE:s + TLOC]
        xtm.append(np.ascontiguousarray(xs))
        xcm.append(np.ascontiguousarray(xs.transpose(0, 2, 1)))
        tabs.append(np.ascontiguousarray(tb))
        masks.append(np.full((128, 1), 0.0 if c == 0 else 1.0, np.float32))

    nc_a = build_a()
    in_a = [{"xcm": xcm[c], "tab": tabs[c], "wk": wkT, "wv": wvT,
             "bias3": bias3} for c in range(NCORES)]
    res_a = run_bass_kernel_spmd(nc_a, in_a, core_ids=list(range(NCORES)))

    kv_all = np.ascontiguousarray(
        np.stack([res_a.results[c]["kvp"] for c in range(NCORES)]))
    ks_all = np.ascontiguousarray(
        np.stack([res_a.results[c]["ksp"][0] for c in range(NCORES)]))

    nc_b = build_b()
    in_b = [{"xcm": xcm[c], "xtm": xtm[c], "tab": tabs[c], "wq": wqT,
             "wr": wrT, "wkf": wkfT, "wvf": wvfT, "bias3": bias3, "gb": gbv,
             "mu2": mu2, "mask": masks[c], "ident": ident,
             "kv_all": kv_all, "ks_in": ks_all} for c in range(NCORES)]
    res_b = run_bass_kernel_spmd(nc_b, in_b, core_ids=list(range(NCORES)))

    out = np.concatenate([res_b.results[c]["y"] for c in range(NCORES)],
                         axis=1)
    return out


# version 07
# speedup vs baseline: 1.0015x; 1.0015x over previous
import numpy as np
import concourse.bass as bass
import concourse.mybir as mybir
from concourse.bass_utils import run_bass_kernel_spmd
from contextlib import ExitStack

F32 = mybir.dt.float32
BF16 = mybir.dt.bfloat16
AF = mybir.ActivationFunctionType
ALU = mybir.AluOpType
AX = mybir.AxisListType

NCORES = 8
B = 2
N = 16384
C = 1024
TLOC = N // NCORES
TILE = 128
NT = TLOC // TILE
TW = TLOC + TILE
SCALE2 = 1.0 / float(N)
INV_C = 1.0 / float(C)
RS_HALF = float(np.sqrt(0.5))


def _even(ap):
    return ap.rearrange("p (i two) -> p i two", two=2)[:, :, 0]


def _odd(ap):
    return ap.rearrange("p (i two) -> p i two", two=2)[:, :, 1]


def _ck(ap):
    return ap.rearrange("p (ck t) -> p ck t", ck=8)


def build_a(ntb=NT):
    nc = bass.Bass("TRN2", target_bir_lowering=False,
                   detect_race_conditions=False)
    xcm = nc.dram_tensor("xcm", [B, C, TW], F32, kind="ExternalInput").ap()
    tab = nc.dram_tensor("tab", [TW, C], F32, kind="ExternalInput").ap()
    wk = nc.dram_tensor("wk", [C, C], F32, kind="ExternalInput").ap()
    wv = nc.dram_tensor("wv", [C, C], F32, kind="ExternalInput").ap()
    bias3 = nc.dram_tensor("bias3", [3, C], F32, kind="ExternalInput").ap()
    kvp = nc.dram_tensor("kvp", [128, 2 * C], F32, kind="ExternalOutput").ap()
    ksp = nc.dram_tensor("ksp", [1, 2 * C], F32, kind="ExternalOutput").ap()

    ctx = ExitStack()
    sb = lambda nm, shp, dt=F32: ctx.enter_context(nc.sbuf_tensor(nm, shp, dt))
    ps = lambda nm, shp, dt=F32: ctx.enter_context(nc.psum_tensor(nm, shp, dt))
    sem = lambda nm: ctx.enter_context(nc.semaphore(nm))

    wk_sb = sb("wk_sb", [128, 8 * C], BF16)
    wv_sb = sb("wv_sb", [128, 8 * C], BF16)
    bk_sb = sb("bk_sb", [1, C], BF16)
    bv_sb = sb("bv_sb", [1, C], BF16)
    ones1 = sb("ones1", [1, 128], BF16)
    ones_f = sb("ones_f", [128, 1], F32)
    xbf = [sb(f"xbf{p}", [128, 8 * TILE], BF16) for p in range(2)]
    tabt = [sb(f"tabt{p}", [128, C], F32) for p in range(2)]
    k_sb = [sb(f"k_sb{p}", [128, C], F32) for p in range(2)]
    kr_bf = [sb(f"kr_bf{p}", [128, C], BF16) for p in range(2)]
    v_bf = [sb(f"v_bf{p}", [128, C], BF16) for p in range(2)]
    t1 = sb("t1", [128, 512], F32)
    t2 = sb("t2", [128, 512], F32)
    ksacc = [sb(f"ksacc{bb}", [128, C], F32) for bb in range(2)]
    kvs_sb = sb("kvs_sb", [128, 2 * C], F32)
    ksp_sb = sb("ksp_sb", [1, 2 * C], F32)

    PK = [ps("PK0", [128, C]), ps("PK1", [128, C])]
    PV = ps("PV", [128, C])
    PKV = ps("PKV", [128, C])

    s_xcE, s_xcO = sem("s_xcE"), sem("s_xcO")
    s_tabE, s_tabO = sem("s_tabE"), sem("s_tabO")
    s_w, s_gc = sem("s_w"), sem("s_gc")
    s_tk, s_tv, s_t2 = sem("s_tk"), sem("s_tv"), sem("s_t2")
    s_gel, s_vev = sem("s_gel"), sem("s_vev")
    s_v1, s_kve = sem("s_v1"), sem("s_kve")
    s_ksf, s_kse = sem("s_ksf"), sem("s_kse")
    s_out = sem("s_out")

    J = 2 * ntb

    xc_sem = lambda j: s_xcE if j % 2 == 0 else s_xcO
    tab_sem = lambda j: s_tabE if j % 2 == 0 else s_tabO

    with nc.Block() as block:
        @block.gpsimd
        def _(g):
            g.dma_start(out=wk_sb[:].rearrange("p (ck d) -> p ck d", ck=8), in_=wk.rearrange("(ck p) d -> p ck d", p=128)).then_inc(s_w, 16)
            g.dma_start(out=wv_sb[:].rearrange("p (ck d) -> p ck d", ck=8), in_=wv.rearrange("(ck p) d -> p ck d", p=128)).then_inc(s_w, 16)
            g.dma_start(out=bk_sb[:], in_=bias3[1:2, :]).then_inc(s_w, 16)
            g.dma_start(out=bv_sb[:], in_=bias3[2:3, :]).then_inc(s_w, 16)
            g.memset(ones1[:], 1.0)
            g.memset(ones_f[:], 1.0).then_inc(s_gc, 1)
            for j in range(J):
                b, i = divmod(j, ntb)
                if j >= 2:
                    g.wait_ge(s_tv, j - 1)
                row = (i + 1) * TILE
                src = xcm[b, :, row:row + TILE].rearrange(
                    "(ck p) t -> p ck t", p=128)
                g.dma_start(out=xbf[j % 2][:].rearrange("p (ck t) -> p ck t", ck=8),
                            in_=src).then_inc(xc_sem(j), 16)

        @block.sync
        def _(sy):
            for j in range(J):
                b, i = divmod(j, ntb)
                if j >= 2:
                    sy.wait_ge(s_v1, j - 1)
                row = (i + 1) * TILE
                sy.dma_start(out=tabt[j % 2][:], in_=tab[row:row + TILE, :]
                             ).then_inc(tab_sem(j), 16)
            sy.wait_ge(s_kve, 2)
            sy.wait_ge(s_kse, 2)
            sy.dma_start(out=kvp[:], in_=kvs_sb[:]).then_inc(s_out, 16)
            sy.dma_start(out=ksp[:], in_=ksp_sb[:]).then_inc(s_out, 16)
            sy.wait_ge(s_out, 32)

        @block.tensor
        def _(t):
            t.wait_ge(s_w, 64)
            t.wait_ge(s_gc, 1)

            def kv_mms(jj):
                bb, ii = divmod(jj, ntb)
                t.wait_ge(s_v1, jj + 1)
                t.wait_ge(s_vev, jj + 1)
                if ii == 0 and bb == 1:
                    t.wait_ge(s_kve, 1)
                for p in range(8):
                    st = (ii == 0) and (p in (0, 4))
                    sp = (ii == ntb - 1) and (p in (3, 7))
                    mm = t.matmul(PKV[:, p * 128:(p + 1) * 128],
                                  kr_bf[jj % 2][:, p * 128:(p + 1) * 128],
                                  v_bf[jj % 2][:, p * 128:(p + 1) * 128],
                                  start=st, stop=sp)
                mm.then_inc(s_t2, 1)

            for j in range(J):
                t.wait_ge(xc_sem(j), 16 * (j // 2 + 1))
                if j >= 2:
                    t.wait_ge(s_gel, j - 1)
                pk = PK[j % 2]
                xb = xbf[j % 2]
                for ck in range(8):
                    for h in range(2):
                        t.matmul(pk[:, h * 512:(h + 1) * 512],
                                 xb[:, ck * 128:(ck + 1) * 128],
                                 wk_sb[:, ck * C + h * 512: ck * C + (h + 1) * 512],
                                 start=(ck == 0), stop=False)
                for h in range(2):
                    mm = t.matmul(pk[:, h * 512:(h + 1) * 512], ones1[:],
                                  bk_sb[0:1, h * 512:(h + 1) * 512],
                                  start=False, stop=True)
                mm.then_inc(s_tk, 1)
                if j >= 1:
                    t.wait_ge(s_vev, j)
                for ck in range(8):
                    for h in range(2):
                        t.matmul(PV[:, h * 512:(h + 1) * 512],
                                 xb[:, ck * 128:(ck + 1) * 128],
                                 wv_sb[:, ck * C + h * 512: ck * C + (h + 1) * 512],
                                 start=(ck == 0), stop=False)
                for h in range(2):
                    mm = t.matmul(PV[:, h * 512:(h + 1) * 512], ones1[:],
                                  bv_sb[0:1, h * 512:(h + 1) * 512],
                                  start=False, stop=True)
                mm.then_inc(s_tv, 1)
                if j >= 1:
                    kv_mms(j - 1)
            kv_mms(J - 1)
            t.wait_ge(s_kve, 2)
            t.wait_ge(s_v1, J)
            for b in range(2):
                if b == 1:
                    t.wait_ge(s_kse, 1)
                for h in range(2):
                    mm = t.matmul(PKV[0:1, h * 512:(h + 1) * 512], ones_f[:],
                                  ksacc[b][:, h * 512:(h + 1) * 512],
                                  start=True, stop=True)
                mm.then_inc(s_ksf, 1)

        @block.scalar
        def _(s):
            for j in range(J):
                s.wait_ge(s_tk, j + 1)
                if j >= 2:
                    s.wait_ge(s_v1, j - 1)
                s.activation(k_sb[j % 2][:], PK[j % 2][:], AF.Gelu
                             ).then_inc(s_gel, 1)
                s.wait_ge(s_tv, j + 1)
                if j >= 2:
                    s.wait_ge(s_t2, j - 1)
                s.copy(v_bf[j % 2][:], PV[:]).then_inc(s_vev, 1)

        @block.vector
        def _(v):
            for j in range(J):
                b, i = divmod(j, ntb)
                v.wait_ge(s_gel, j + 1)
                v.wait_ge(tab_sem(j), 16 * (j // 2 + 1))
                if j >= 2:
                    v.wait_ge(s_t2, j - 1)
                ks = k_sb[j % 2]
                if i == 0:
                    v.tensor_copy(ksacc[b][:], ks[:])
                else:
                    v.tensor_add(ksacc[b][:], ksacc[b][:], ks[:])
                cosv = tabt[j % 2][:, 0:512]
                sinv = tabt[j % 2][:, 512:1024]
                kr = kr_bf[j % 2]
                v.scalar_tensor_tensor(t1[:], _even(ks[:]), 0.21, cosv,
                                       ALU.add, ALU.mult)
                v.scalar_tensor_tensor(t2[:], _odd(ks[:]), 0.21, sinv,
                                       ALU.add, ALU.mult)
                v.drain()
                v.tensor_sub(_even(kr[:]), t1[:], t2[:])
                v.drain()
                v.scalar_tensor_tensor(t1[:], _odd(ks[:]), 0.21, cosv,
                                       ALU.add, ALU.mult)
                v.scalar_tensor_tensor(t2[:], _even(ks[:]), 0.21, sinv,
                                       ALU.add, ALU.mult)
                v.drain()
                v.tensor_add(_odd(kr[:]), t1[:], t2[:]).then_inc(s_v1, 1)
                if i == ntb - 1:
                    v.wait_ge(s_t2, 16 * (b + 1))
                    v.tensor_scalar(kvs_sb[:, b * C:(b + 1) * C], PKV[:],
                                    SCALE2, None, ALU.mult).then_inc(s_kve, 1)
            for b in range(2):
                v.wait_ge(s_ksf, b + 1)
                v.tensor_copy(ksp_sb[0:1, b * C:(b + 1) * C], PKV[0:1, :]).then_inc(s_kse, 1)

    ctx.close()
    return nc


def build_b(tj=None, debug=False):
    nc = bass.Bass("TRN2", target_bir_lowering=False,
                   detect_race_conditions=False)
    xcm = nc.dram_tensor("xcm", [B, C, TW], F32, kind="ExternalInput").ap()
    xtm = nc.dram_tensor("xtm", [B, TW, C], F32, kind="ExternalInput").ap()
    tab = nc.dram_tensor("tab", [TW, C], F32, kind="ExternalInput").ap()
    wq = nc.dram_tensor("wq", [C, C], F32, kind="ExternalInput").ap()
    wr = nc.dram_tensor("wr", [C, C], F32, kind="ExternalInput").ap()
    wkf = nc.dram_tensor("wkf", [C, C], F32, kind="ExternalInput").ap()
    wvf = nc.dram_tensor("wvf", [C, C], F32, kind="ExternalInput").ap()
    bias3 = nc.dram_tensor("bias3", [3, C], F32, kind="ExternalInput").ap()
    gb = nc.dram_tensor("gb", [2, C], F32, kind="ExternalInput").ap()
    mu2 = nc.dram_tensor("mu2", [128, 16], F32, kind="ExternalInput").ap()
    mask = nc.dram_tensor("mask", [128, 1], F32, kind="ExternalInput").ap()
    ident = nc.dram_tensor("ident", [128, 128], F32, kind="ExternalInput").ap()
    kv_all = nc.dram_tensor("kv_all", [NCORES, 128, 2 * C], F32,
                            kind="ExternalInput").ap()
    ks_in = nc.dram_tensor("ks_in", [NCORES, 2 * C], F32,
                           kind="ExternalInput").ap()
    y = nc.dram_tensor("y", [B, TLOC, C], F32, kind="ExternalOutput").ap()
    dbg = (nc.dram_tensor("dbg", [8, 128, C], F32, kind="ExternalOutput").ap()
           if debug else None)
    dbgs = (nc.dram_tensor("dbgs", [128, 16 + 16 + 8 + 1040], F32,
                           kind="ExternalOutput").ap() if debug else None)
    km_dr = nc.dram_tensor("km_dr", [1, 2 * C], F32, kind="Internal").ap()
    ch_dr = nc.dram_tensor("ch_dr", [1, 32], F32, kind="Internal").ap()

    ctx = ExitStack()
    sb = lambda nm, shp, dt=F32: ctx.enter_context(nc.sbuf_tensor(nm, shp, dt))
    ps = lambda nm, shp, dt=F32: ctx.enter_context(nc.psum_tensor(nm, shp, dt))
    sem = lambda nm: ctx.enter_context(nc.semaphore(nm))

    wq_sb = sb("wq_sb", [128, 8 * C], BF16)
    wr_sb = sb("wr_sb", [128, 8 * C], BF16)
    wkf_sb = sb("wkf_sb", [128, 8 * C], BF16)
    wvf_sb = sb("wvf_sb", [128, 8 * C], BF16)
    bq_sb = sb("bq_sb", [1, C], BF16)
    ones1 = sb("ones1", [1, 128], BF16)
    identb = sb("identb", [128, 128], BF16)
    identf = sb("identf", [128, 128], F32)
    mu_sb = sb("mu_sb", [128, 16], F32)
    mask_sb = sb("mask_sb", [128, 1], F32)
    kvacc = sb("kvacc", [128, 2 * C], F32)
    kv_bd = sb("kv_bd", [128, 2 * C], BF16)
    kmb = sb("kmb", [128, 2 * C], F32)
    chb = sb("chb", [128, 32], F32)
    gamb = sb("gamb", [128, C], F32)
    betb = sb("betb", [128, C], F32)

    xbf = [sb(f"xbf{p}", [128, 8 * TILE], BF16) for p in range(2)]
    tabt = [sb(f"tabt{p}", [128, C], F32) for p in range(2)]
    xts = [sb(f"xts{p}", [128, C], F32) for p in range(2)]
    q_sb = sb("q_sb", [128, C], F32)
    zt_sb = sb("zt_sb", [128, C], F32)
    z16 = sb("z16", [128, 16], F32)
    qr_bf = [sb(f"qr_bf{p}", [128, C], BF16) for p in range(2)]
    qrT_sb = [sb(f"qrT_sb{p}", [128, C], BF16) for p in range(2)]
    t1 = sb("t1", [128, 512], F32)
    t2 = sb("t2", [128, 512], F32)
    res1 = sb("res1", [128, C], F32)
    qro_sb = res1
    y1_sb = sb("y1_sb", [128, C], F32)
    xb_sb = sb("xb_sb", [128, C], F32)
    u_sb = xb_sb
    tt_sb = sb("tt_sb", [128, C], F32)
    fo_sb = tt_sb
    res2 = sb("res2", [128, C], F32)
    x1_tm = [sb(f"x1_tm{p}", [128, C], F32) for p in range(2)]
    ring = [sb("ring0", [128, 8 * 130], F32), sb("ring1", [128, 8 * 130], F32)]
    d_sb = sb("d_sb", [128, C], F32)
    u2_sb = sb("u2_sb", [128, C], F32)
    xr_bf = sb("xr_bf", [128, C], BF16)
    xk_bf = sb("xk_bf", [128, C], BF16)
    tr_sb = sb("tr_sb", [128, C], F32)
    kact_bf = sb("kact_bf", [128, C], BF16)
    kactT_sb = sb("kactT_sb", [128, C], BF16)
    res3 = sb("res3", [128, C], F32)
    y_sb = [kvacc[:, 0:C], kvacc[:, C:2 * C]]
    dump = q_sb
    krelu = q_sb
    sumst = sb("sumst", [128, 4], F32)
    sqst = sb("sqst", [128, 4], F32)
    stt_ = sb("stt_", [128, 16], F32)
    st2_ = sb("st2_", [128, 2], F32)
    eps12 = sb("eps12", [128, 1], F32)
    onez = sb("onez", [128, 16], F32)
    eps5 = sb("eps5", [128, 1], F32)

    PQ = ps("PQ", [128, C])
    PTQ = ps("PTQ", [128, C], BF16)
    PTK = ps("PTK", [128, C], BF16)
    PO = ps("PO", [128, C])
    PX = ps("PX", [128, C])

    names = ["sVZ", "sTQ", "sSG", "sTT", "sEQ", "sTO", "sR1", "sS1", "sVa1",
             "sSb1", "sVc1", "sS2", "sR2", "sS3", "sVa2", "sSb2", "sVc2",
             "sS4", "sTX", "sVR", "sVM", "sGM", "sTR", "sSR", "sTK", "sSKr",
             "sSK", "sTKT", "sEK", "sTV", "sVG", "sS5", "sVa3", "sSb3",
             "sVc3", "sS6"]
    S = {nm: sem(nm) for nm in names}
    s_xcE, s_xcO = sem("s_xcE"), sem("s_xcO")
    s_tabE, s_tabO = sem("s_tabE"), sem("s_tabO")
    s_xtE, s_xtO = sem("s_xtE"), sem("s_xtO")
    s_w = sem("s_w")
    s_w2 = sem("s_w2")
    s_acc = sem("s_acc")
    s_ack = sem("s_ack")
    s_set = sem("s_set")
    s_kmb = sem("s_kmb")
    s_gc = sem("s_gc")
    s_y = sem("s_y")
    s_vd = sem("s_vd")
    s_sd = sem("s_sd")
    s_gd = sem("s_gd")
    sVC = sem("sVC")
    sGX = sem("sGX")
    sGZ = sem("sGZ")
    sVT = sem("sVT")

    def stc(ln, k):
        return stt_[:, ln * 5 + k: ln * 5 + k + 1]

    mur_view = mu_sb[:, 0:8].unsqueeze(2).broadcast_to([128, 8, 128])
    muk_view = mu_sb[:, 8:16].unsqueeze(2).broadcast_to([128, 8, 128])
    ring_v = [ring[b][:].rearrange("p (ck w) -> p ck w", w=130) for b in range(2)]
    x_v = [ring_v[b][:, :, 1:129] for b in range(2)]
    xp_v = [ring_v[b][:, :, 0:128] for b in range(2)]
    colL = [ring_v[b][:, :, 128:129] for b in range(2)]
    col0 = [ring_v[b][:, :, 0:1] for b in range(2)]

    xc_sem = lambda j: s_xcE if j % 2 == 0 else s_xcO
    tab_sem = lambda j: s_tabE if j % 2 == 0 else s_tabO
    xt_sem = lambda j: s_xtE if j % 2 == 0 else s_xtO

    tiles = [(b, i) for i in range(NT + 1) for b in range(2)]
    if tj is not None:
        tiles = tiles[:tj]
    real_ord = {}
    ro = 0
    for j, (b, i) in enumerate(tiles):
        if i > 0:
            real_ord[j] = ro
            ro += 1
    NREAL = ro
    J = len(tiles)

    with nc.Block() as block:
        @block.gpsimd
        def _(g):
            gdc = [0]

            def gsync():
                g.drain()

            g.dma_start(out=wq_sb[:].rearrange("p (ck d) -> p ck d", ck=8), in_=wq.rearrange("(ck p) d -> p ck d", p=128)).then_inc(s_w, 16)
            g.dma_start(out=wr_sb[:].rearrange("p (ck d) -> p ck d", ck=8), in_=wr.rearrange("(ck p) d -> p ck d", p=128)).then_inc(s_w, 16)
            g.dma_start(out=wkf_sb[:].rearrange("p (ck d) -> p ck d", ck=8), in_=wkf.rearrange("(ck p) d -> p ck d", p=128)).then_inc(s_w, 16)
            g.dma_start(out=wvf_sb[:].rearrange("p (ck d) -> p ck d", ck=8), in_=wvf.rearrange("(ck p) d -> p ck d", p=128)).then_inc(s_w, 16)
            g.dma_start(out=bq_sb[:], in_=bias3[0:1, :]).then_inc(s_w, 16)
            g.dma_start(out=identb[:], in_=ident[:]).then_inc(s_w, 16)
            g.dma_start(out=gamb[:], in_=gb[0:1, :].broadcast_to([128, C])).then_inc(s_w, 16)
            g.dma_start(out=betb[:], in_=gb[1:2, :].broadcast_to([128, C])).then_inc(s_w, 16)
            g.memset(ones1[:], 1.0)
            g.memset(onez[:], 1.0)
            g.memset(eps12[:], 1e-12)
            g.memset(eps5[:], 1e-5).then_inc(s_gc, 1)
            for j in (0, 1)[:len(tiles)]:
                b, i = tiles[j]
                src = xcm[b, :, i * TILE:(i + 1) * TILE].rearrange(
                    "(ck p) t -> p ck t", p=128)
                g.dma_start(out=xbf[j % 2][:].rearrange("p (ck t) -> p ck t", ck=8),
                            in_=src).then_inc(xc_sem(j), 16)
            g.dma_start(out=kvacc[:], in_=kv_all[0]).then_inc(s_acc, 16)
            for k in range(1, NCORES):
                g.wait_ge(s_acc, 16 * k)
                g.dma_start(out=kvacc[:], in_=kv_all[k], accum_op=ALU.add
                            ).then_inc(s_acc, 16)
            g.dma_start(out=kmb[0:1, :], in_=ks_in[0:1, :]).then_inc(s_ack, 16)
            for k in range(1, NCORES):
                g.wait_ge(s_ack, 16 * k)
                g.dma_start(out=kmb[0:1, :], in_=ks_in[k:k + 1, :],
                            accum_op=ALU.add).then_inc(s_ack, 16)
            g.wait_ge(s_set, 1)
            g.dma_start(out=km_dr[:], in_=kmb[0:1, :]).then_inc(s_kmb, 16)
            g.dma_start(out=ch_dr[:], in_=chb[0:1, 0:32]).then_inc(s_kmb, 16)
            g.wait_ge(s_kmb, 32)
            g.dma_start(out=kmb[:, 0:C], in_=km_dr[0:1, 0:C].broadcast_to([128, C])).then_inc(s_kmb, 16)
            g.dma_start(out=kmb[:, C:2 * C], in_=km_dr[0:1, C:2 * C].broadcast_to([128, C])).then_inc(s_kmb, 16)
            g.dma_start(out=chb[:, 0:16], in_=ch_dr[0:1, 0:16].broadcast_to([128, 16])).then_inc(s_kmb, 16)
            g.dma_start(out=chb[:, 16:32], in_=ch_dr[0:1, 16:32].broadcast_to([128, 16])).then_inc(s_kmb, 16)
            for j, (b, i) in enumerate(tiles):
                if j >= 2:
                    g.wait_ge(S["sTQ"], j - 1)
                    src = xcm[b, :, i * TILE:(i + 1) * TILE].rearrange(
                        "(ck p) t -> p ck t", p=128)
                    g.dma_start(out=xbf[j % 2][:].rearrange("p (ck t) -> p ck t", ck=8),
                                in_=src).then_inc(xc_sem(j), 16)
                if j == 0:
                    g.wait_ge(s_w, 16 * 8)
                g.wait_ge(xt_sem(j), 16 * (j // 2 + 1))
                if j >= 1:
                    g.wait_ge(S["sR2"], j)
                g.tensor_add(u2_sb[:], xts[j % 2][:], betb[:]).then_inc(sGX, 1)

                if i > 0:
                    g.wait_ge(S["sVM"], j + 1)
                    g.wait_ge(S["sTK"], j)
                    g.tensor_tensor(_ck(d_sb[:]), _ck(d_sb[:]), muk_view,
                                    ALU.mult)
                    gsync()
                    g.tensor_tensor(_ck(xk_bf[:]), xp_v[b], _ck(d_sb[:]),
                                    ALU.add).then_inc(S["sGM"], 1)
                else:
                    g.nop().then_inc(S["sGM"], 1)
            if debug:
                jl = len(tiles)
                for nm in ("sVR", "sS4", "sEQ", "sVZ", "sR2", "sVc1", "sVc2",
                           "sSG", "sS2", "sTO"):
                    g.wait_ge(S[nm], jl)
                g.dma_start(out=dbg[0], in_=qz_sb[:]).then_inc(s_kmb, 16)
                g.dma_start(out=dbg[1], in_=qr_bf[(jl - 1) % 2][:]).then_inc(s_kmb, 16)
                g.dma_start(out=dbg[2], in_=qrT_sb[(jl - 1) % 2][:]).then_inc(s_kmb, 16)
                g.dma_start(out=dbg[3], in_=res1[:]).then_inc(s_kmb, 16)
                g.dma_start(out=dbg[4], in_=y1_sb[:]).then_inc(s_kmb, 16)
                g.dma_start(out=dbg[5], in_=res2[:]).then_inc(s_kmb, 16)
                g.dma_start(out=dbg[6], in_=x1_tm[(jl - 1) % 2][:]).then_inc(s_kmb, 16)
                g.dma_start(out=dbg[7], in_=tt_sb[:]).then_inc(s_kmb, 16)
                g.dma_start(out=dbgs[:, 0:16], in_=z16[:]).then_inc(s_kmb, 16)
                g.dma_start(out=dbgs[:, 16:32], in_=stt_[:]).then_inc(s_kmb, 16)
                g.dma_start(out=dbgs[:, 32:36], in_=sumst[:]).then_inc(s_kmb, 16)
                g.dma_start(out=dbgs[:, 36:40], in_=sqst[:]).then_inc(s_kmb, 16)
                g.dma_start(out=dbgs[:, 40:1080], in_=ring[tiles[-1][0]][:]).then_inc(s_kmb, 16)

        @block.sync
        def _(sy):
            sy.dma_start(out=identf[:], in_=ident[:]).then_inc(s_w2, 16)
            sy.dma_start(out=mu_sb[:], in_=mu2[:]).then_inc(s_w2, 16)
            sy.dma_start(out=mask_sb[:], in_=mask[:]).then_inc(s_w2, 16)
            for j, (b, i) in enumerate(tiles):
                row = i * TILE
                if j >= 2:
                    sy.wait_ge(S["sVZ"], j - 1)
                sy.dma_start(out=tabt[j % 2][:], in_=tab[row:row + TILE, :]
                             ).then_inc(tab_sem(j), 16)
                if j >= 2:
                    sy.wait_ge(S["sR2"], j - 1)
                    sy.wait_ge(sGX, j - 1)
                sy.dma_start(out=xts[j % 2][:], in_=xtm[b, row:row + TILE, :]
                             ).then_inc(xt_sem(j), 16)
                if i > 0:
                    sy.wait_ge(S["sS6"], j + 1)
                    sy.dma_start(out=y[b, (i - 1) * TILE:i * TILE, :],
                                 in_=y_sb[j % 2]).then_inc(s_y, 16)
            if NREAL > 0:
                sy.wait_ge(s_y, 16 * NREAL)
            if debug:
                sy.wait_ge(s_kmb, 16 * 19)

        @block.tensor
        def _(t):
            t.wait_ge(s_w, 16 * 8)
            t.wait_ge(s_gc, 1)
            t.wait_ge(s_w2, 16 * 3)
            for j, (b, i) in enumerate(tiles):
                real = i > 0
                t.wait_ge(xc_sem(j), 16 * (j // 2 + 1))
                if j >= 1:
                    t.wait_ge(S["sSR"], j)
                xbfj = xbf[j % 2]
                for ck in range(8):
                    for h in range(2):
                        t.matmul(PQ[:, h * 512:(h + 1) * 512],
                                 xbfj[:, ck * 128:(ck + 1) * 128],
                                 wq_sb[:, ck * C + h * 512: ck * C + (h + 1) * 512],
                                 start=(ck == 0), stop=False)
                for h in range(2):
                    mm = t.matmul(PQ[:, h * 512:(h + 1) * 512], ones1[:],
                                  bq_sb[0:1, h * 512:(h + 1) * 512],
                                  start=False, stop=True)
                mm.then_inc(S["sTQ"], 1)
                t.wait_ge(S["sVZ"], j + 1)
                if j >= 1:
                    t.wait_ge(S["sEQ"], j)
                for ck in range(8):
                    mm = t.transpose(PTQ[:, ck * 128:(ck + 1) * 128],
                                     qr_bf[j % 2][:, ck * 128:(ck + 1) * 128],
                                     identb[:])
                mm.then_inc(S["sTT"], 1)
                t.wait_ge(S["sEQ"], j + 1)
                if j >= 1:
                    t.wait_ge(S["sSKr"], j)
                if j == 0:
                    t.wait_ge(s_set, 2)
                for p in range(8):
                    mm = t.matmul(PO[:, p * 128:(p + 1) * 128],
                                  qrT_sb[j % 2][:, p * 128:(p + 1) * 128],
                                  kv_bd[:, b * C + p * 128: b * C + (p + 1) * 128],
                                  start=(p in (0, 4)), stop=(p in (3, 7)))
                mm.then_inc(S["sTO"], 1)
                t.wait_ge(S["sS4"], j + 1)
                if j >= 1:
                    t.wait_ge(S["sVG"], j)
                for ck in range(8):
                    mm = t.transpose(PX[:, ck * 128:(ck + 1) * 128],
                                     x1_tm[j % 2][:, ck * 128:(ck + 1) * 128],
                                     identf[:])
                mm.then_inc(S["sTX"], 1)
                if real:
                    t.wait_ge(S["sVM"], j + 1)
                    t.wait_ge(S["sSG"], j + 1)
                    for ck in range(8):
                        for h in range(2):
                            mm = t.matmul(PQ[:, h * 512:(h + 1) * 512],
                                          xr_bf[:, ck * 128:(ck + 1) * 128],
                                          wr_sb[:, ck * C + h * 512: ck * C + (h + 1) * 512],
                                          start=(ck == 0), stop=(ck == 7))
                    mm.then_inc(S["sTR"], 1)
                    t.wait_ge(S["sGM"], j + 1)
                    t.wait_ge(S["sR1"], j + 1)
                    for ck in range(8):
                        for h in range(2):
                            mm = t.matmul(PO[:, h * 512:(h + 1) * 512],
                                          xk_bf[:, ck * 128:(ck + 1) * 128],
                                          wkf_sb[:, ck * C + h * 512: ck * C + (h + 1) * 512],
                                          start=(ck == 0), stop=(ck == 7))
                    mm.then_inc(S["sTK"], 1)
                    t.wait_ge(S["sSK"], j + 1)
                    if j >= 1:
                        t.wait_ge(S["sEK"], j)
                    for ck in range(8):
                        mm = t.transpose(PTK[:, ck * 128:(ck + 1) * 128],
                                         kact_bf[:, ck * 128:(ck + 1) * 128],
                                         identb[:])
                    mm.then_inc(S["sTKT"], 1)
                    t.wait_ge(S["sEK"], j + 1)
                    t.wait_ge(S["sVR"], j + 1)
                    for ck in range(8):
                        for h in range(2):
                            mm = t.matmul(PX[:, h * 512:(h + 1) * 512],
                                          kactT_sb[:, ck * 128:(ck + 1) * 128],
                                          wvf_sb[:, ck * C + h * 512: ck * C + (h + 1) * 512],
                                          start=(ck == 0), stop=(ck == 7))
                    mm.then_inc(S["sTV"], 1)
                else:
                    if j >= 1:
                        t.wait_ge(S["sEK"], j)
                    for nm in ("sTR", "sTK", "sTKT", "sTV"):
                        t.transpose(PTK[:, 0:128], identb[:], identb[:]
                                    ).then_inc(S[nm], 1)

        @block.scalar
        def _(s):
            sdc = [0]

            def ssync():
                s.drain()

            s.wait_ge(s_gc, 1)
            for j, (b, i) in enumerate(tiles):
                real = i > 0
                s.wait_ge(S["sTQ"], j + 1)
                if j >= 1:
                    s.wait_ge(S["sVZ"], j)
                s.activation(q_sb[:], PQ[:], AF.Gelu).then_inc(S["sSG"], 1)
                s.wait_ge(S["sTT"], j + 1)
                if j >= 2:
                    s.wait_ge(S["sTO"], j - 1)
                s.copy(qrT_sb[j % 2][:], PTQ[:]).then_inc(S["sEQ"], 1)
                s.wait_ge(S["sR1"], j + 1)
                s.activation(dump[:], res1[:], AF.Square,
                             accum_out=sqst[:, 0:1])
                s.activation(stc(0, 0), sumst[:, 0:1], AF.Square, scale=INV_C
                             ).then_inc(S["sS1"], 1)
                s.wait_ge(S["sVa1"], j + 1)
                s.activation(stc(0, 2), stc(0, 1), AF.Sqrt,
                             bias=eps12[:, 0:1]).then_inc(S["sSb1"], 1)
                s.wait_ge(S["sVc1"], j + 1)
                if j >= 1:
                    s.wait_ge(S["sR2"], j)
                s.activation(y1_sb[:], res1[:], AF.Identity, bias=stc(0, 4),
                             scale=stc(0, 3)).then_inc(S["sS2"], 1)
                s.wait_ge(S["sR2"], j + 1)
                s.activation(dump[:], res2[:], AF.Square,
                             accum_out=sqst[:, 1:2])
                s.activation(stc(1, 0), sumst[:, 1:2], AF.Square, scale=INV_C
                             ).then_inc(S["sS3"], 1)
                s.wait_ge(S["sVa2"], j + 1)
                s.activation(stc(1, 2), stc(1, 1), AF.Sqrt,
                             bias=eps5[:, 0:1]).then_inc(S["sSb2"], 1)
                s.wait_ge(S["sVc2"], j + 1)
                if j >= 1:
                    s.wait_ge(S["sVG"], j)
                s.activation(x1_tm[j % 2][:], res2[:], AF.Identity, bias=stc(1, 4),
                             scale=stc(1, 3)).then_inc(S["sS4"], 1)
                s.wait_ge(sVC, j + 1)
                s.wait_ge(S["sTX"], j + 1)
                s.wait_ge(S["sGM"], j)
                if real:
                    s.copy(x_v[b], _ck(PX[:])).then_inc(S["sVR"], 1)
                else:
                    s.copy(x_v[b], _ck(PX[:]))
                    ssync()
                    s.activation(colL[b].squeeze(2), colL[b].squeeze(2),
                                 AF.Identity,
                                 scale=mask_sb[:, 0:1]).then_inc(S["sVR"], 1)
                if real:
                    s.wait_ge(S["sTR"], j + 1)
                    s.wait_ge(S["sVG"], j)
                    s.activation(tr_sb[:], PQ[:], AF.Tanh, scale=0.5
                                 ).then_inc(S["sSR"], 1)
                    s.wait_ge(S["sTK"], j + 1)
                    s.activation(krelu[:], PO[:], AF.Relu).then_inc(S["sSKr"], 1)
                    ssync()
                    s.wait_ge(S["sTKT"], j)
                    s.activation(kact_bf[:], krelu[:], AF.Square, scale=RS_HALF
                                 ).then_inc(S["sSK"], 1)
                    s.wait_ge(S["sTKT"], j + 1)
                    s.wait_ge(S["sTV"], j)
                    s.copy(kactT_sb[:], PTK[:]).then_inc(S["sEK"], 1)
                    s.wait_ge(S["sVG"], j + 1)
                    s.activation(dump[:], res3[:], AF.Square,
                                 accum_out=sqst[:, 2:3])
                    s.activation(stc(2, 0), sumst[:, 2:3], AF.Square, scale=INV_C
                                 ).then_inc(S["sS5"], 1)
                    s.wait_ge(S["sVa3"], j + 1)
                    s.activation(stc(2, 2), stc(2, 1), AF.Sqrt,
                                 bias=eps5[:, 0:1]).then_inc(S["sSb3"], 1)
                    s.wait_ge(S["sVc3"], j + 1)
                    ord_ = real_ord[j]
                    if ord_ >= 2:
                        s.wait_ge(s_y, 16 * (ord_ - 1))
                    s.activation(y_sb[j % 2], res3[:], AF.Identity,
                                 bias=stc(2, 4), scale=stc(2, 3)
                                 ).then_inc(S["sS6"], 1)
                else:
                    for nm in ("sSR", "sSKr", "sSK", "sEK", "sS5", "sSb3",
                               "sS6"):
                        s.nop().then_inc(S[nm], 1)

        @block.vector
        def _(v):
            vdc = [0]

            def vsync():
                v.drain()

            v.memset(ring[0][:], 0.0)
            v.memset(ring[1][:], 0.0)
            if debug:
                v.memset(stt_[:], 0.0)
                v.memset(sumst[:], 0.0)
                v.memset(sqst[:], 0.0)
            v.wait_ge(s_acc, 16 * NCORES)
            v.wait_ge(s_ack, 16 * NCORES)
            v.wait_ge(s_w2, 16 * 3)
            v.tensor_scalar(kmb[0:1, :], kmb[0:1, :], 1.0 / float(N), 0.21,
                            ALU.mult, ALU.add)
            vsync()
            v.reduce_sum(chb[0:1, 0:32],
                         kmb[0:1, :].rearrange("p (h e) -> p h e", h=32),
                         axis=AX.X)
            vsync()
            v.tensor_scalar(chb[0:1, 0:32], chb[0:1, 0:32], 0.21, None,
                            ALU.mult).then_inc(s_set, 1)
            v.memset(kv_bd[:], 0.0)
            vsync()
            for bb in range(2):
                for p in range(8):
                    off = bb * C + p * 128
                    v.tensor_copy(kv_bd[0:64, off:off + 64],
                                  kvacc[0:64, off:off + 64])
                    v.tensor_copy(kv_bd[64:128, off + 64:off + 128],
                                  kvacc[64:128, off + 64:off + 128])
            v.nop().then_inc(s_set, 1)
            v.wait_ge(s_kmb, 16 * 6)
            v.tensor_scalar(chb[:], chb[:], 1e-6, None, ALU.add
                            ).then_inc(s_set, 1)
            for j, (b, i) in enumerate(tiles):
                real = i > 0
                v.wait_ge(S["sSG"], j + 1)
                v.wait_ge(tab_sem(j), 16 * (j // 2 + 1))
                if j == 0:
                    v.wait_ge(s_kmb, 16 * 6)
                if j == 0:
                    v.wait_ge(s_set, 3)
                v.tensor_mul(zt_sb[:], q_sb[:], kmb[:, b * C:(b + 1) * C])
                v.reduce_sum(z16[:], zt_sb[:].rearrange("p (h e) -> p h e", h=16),
                             axis=AX.X)
                vsync()
                v.tensor_add(z16[:], z16[:], chb[:, b * 16:(b + 1) * 16])
                vsync()
                v.reciprocal(z16[:], z16[:])
                vsync()
                cosv = tabt[j % 2][:, 0:512]
                sinv = tabt[j % 2][:, 512:1024]
                if j >= 2:
                    v.wait_ge(S["sTT"], j - 1)
                if j >= 1:
                    v.wait_ge(S["sS2"], j)
                qr = qr_bf[j % 2]
                v.scalar_tensor_tensor(t1[:], _even(q_sb[:]), 0.21, cosv,
                                       ALU.add, ALU.mult)
                v.scalar_tensor_tensor(t2[:], _odd(q_sb[:]), 0.21, sinv,
                                       ALU.add, ALU.mult)
                v.tensor_sub(_even(qro_sb[:]), t1[:], t2[:])
                v.scalar_tensor_tensor(t1[:], _odd(q_sb[:]), 0.21, cosv,
                                       ALU.add, ALU.mult)
                v.scalar_tensor_tensor(t2[:], _even(q_sb[:]), 0.21, sinv,
                                       ALU.add, ALU.mult)
                v.tensor_add(_odd(qro_sb[:]), t1[:], t2[:])
                vsync()
                zview = z16[:].unsqueeze(2).broadcast_to([128, 16, 64])
                v.tensor_tensor(qr[:].rearrange("p (h e) -> p h e", h=16),
                                qro_sb[:].rearrange("p (h e) -> p h e", h=16),
                                zview, ALU.mult).then_inc(S["sVZ"], 1)
                v.wait_ge(S["sTO"], j + 1)
                v.wait_ge(xt_sem(j), 16 * (j // 2 + 1))
                if j >= 1:
                    v.wait_ge(S["sS2"], j)
                v.scalar_tensor_tensor(res1[:], PO[:], 1.0, xts[j % 2][:],
                                       ALU.mult, ALU.add,
                                       accum_out=sumst[:, 0:1]).then_inc(S["sR1"], 1)
                v.wait_ge(S["sS1"], j + 1)
                v.tensor_scalar(stc(0, 1), sqst[:, 0:1], INV_C, stc(0, 0),
                                ALU.mult, ALU.subtract).then_inc(S["sVa1"], 1)
                v.wait_ge(S["sSb1"], j + 1)
                v.reciprocal(stc(0, 3), stc(0, 2))
                vsync()
                v.tensor_scalar(stc(0, 4), sumst[:, 0:1], stc(0, 3), -INV_C,
                                ALU.mult, ALU.mult).then_inc(S["sVc1"], 1)
                v.wait_ge(S["sS2"], j + 1)
                v.wait_ge(sGX, j + 1)
                if j >= 1:
                    v.wait_ge(S["sS4"], j)
                v.tensor_mul(tt_sb[:], y1_sb[:], gamb[:])
                v.scalar_tensor_tensor(res2[:], tt_sb[:], 1.0, u2_sb[:],
                                       ALU.mult, ALU.add,
                                       accum_out=sumst[:, 1:2]).then_inc(S["sR2"], 1)
                v.wait_ge(S["sS3"], j + 1)
                v.tensor_scalar(stc(1, 1), sqst[:, 1:2], INV_C, stc(1, 0),
                                ALU.mult, ALU.subtract).then_inc(S["sVa2"], 1)
                v.wait_ge(S["sSb2"], j + 1)
                v.reciprocal(stc(1, 3), stc(1, 2))
                vsync()
                v.tensor_scalar(stc(1, 4), sumst[:, 1:2], stc(1, 3), -INV_C,
                                ALU.mult, ALU.mult).then_inc(S["sVc2"], 1)
                if j >= 1:
                    v.wait_ge(S["sVR"], j)
                    v.wait_ge(S["sGM"], j)
                v.tensor_copy(col0[b], colL[b]).then_inc(sVC, 1)
                if real:
                    v.wait_ge(S["sVR"], j + 1)
                    v.wait_ge(S["sTR"], j)
                    v.tensor_sub(_ck(d_sb[:]), x_v[b], xp_v[b])
                    v.tensor_mul(_ck(u_sb[:]), _ck(d_sb[:]), mur_view)
                    v.tensor_add(_ck(xr_bf[:]), xp_v[b], _ck(u_sb[:])
                                 ).then_inc(S["sVM"], 1)
                    v.wait_ge(S["sTV"], j + 1)
                    v.wait_ge(S["sSR"], j + 1)
                    if j >= 1:
                        v.wait_ge(S["sS6"], j)
                    v.scalar_tensor_tensor(fo_sb[:], tr_sb[:], 1.0, PX[:],
                                           ALU.add, ALU.mult)
                    v.scalar_tensor_tensor(res3[:], fo_sb[:], 1.0,
                                           x1_tm[j % 2][:], ALU.mult, ALU.add,
                                           accum_out=sumst[:, 2:3]
                                           ).then_inc(S["sVG"], 1)
                    v.wait_ge(S["sS5"], j + 1)
                    v.tensor_scalar(stc(2, 1), sqst[:, 2:3], INV_C, stc(2, 0),
                                    ALU.mult, ALU.subtract).then_inc(S["sVa3"], 1)
                    v.wait_ge(S["sSb3"], j + 1)
                    v.reciprocal(stc(2, 3), stc(2, 2))
                    vsync()
                    v.tensor_scalar(stc(2, 4), sumst[:, 2:3], stc(2, 3), -INV_C,
                                    ALU.mult, ALU.mult).then_inc(S["sVc3"], 1)
                else:
                    for nm in ("sVM", "sVG", "sVa3", "sVc3"):
                        v.nop().then_inc(S[nm], 1)

    ctx.close()
    return nc


def kernel(L0_attn_x, L0_attn_Wq, L0_attn_bq, L0_attn_Wk, L0_attn_bk,
           L0_attn_Wv, L0_attn_bv, L0_attn_gamma, L0_attn_beta,
           L0_ffn_W_r, L0_ffn_W_k, L0_ffn_W_v, L0_ffn_mu_r, L0_ffn_mu_k):
    x = np.ascontiguousarray(np.asarray(L0_attn_x, dtype=np.float32))
    wqT = np.ascontiguousarray(np.asarray(L0_attn_Wq, np.float32).T)
    wkT = np.ascontiguousarray(np.asarray(L0_attn_Wk, np.float32).T)
    wvT = np.ascontiguousarray(np.asarray(L0_attn_Wv, np.float32).T)
    wrT = np.ascontiguousarray(np.asarray(L0_ffn_W_r, np.float32).T)
    wkfT = np.ascontiguousarray(np.asarray(L0_ffn_W_k, np.float32).T)
    wvfT = np.ascontiguousarray(np.asarray(L0_ffn_W_v, np.float32).T)
    bias3 = np.ascontiguousarray(np.stack([
        np.asarray(L0_attn_bq, np.float32),
        np.asarray(L0_attn_bk, np.float32),
        np.asarray(L0_attn_bv, np.float32)]))
    gbv = np.ascontiguousarray(np.stack([
        np.asarray(L0_attn_gamma, np.float32),
        np.asarray(L0_attn_beta, np.float32)]))
    mu2 = np.ascontiguousarray(np.concatenate([
        np.asarray(L0_ffn_mu_r, np.float32).reshape(8, 128).T,
        np.asarray(L0_ffn_mu_k, np.float32).reshape(8, 128).T], axis=1))
    ident = np.eye(128, dtype=np.float32)

    import jax
    import jax.numpy as jnp
    with jax.default_device(jax.devices("cpu")[0]):
        ivj = 1.0 / jnp.float32(10000.0) ** (
            jnp.arange(0, C, 2, dtype=jnp.float32) / C)
        angj = jnp.arange(N, dtype=jnp.float32)[:, None] * ivj[None, :]
        tab_full = np.ascontiguousarray(np.concatenate(
            [np.asarray(jnp.cos(angj)), np.asarray(jnp.sin(angj))],
            axis=1).astype(np.float32))

    xcm, xtm, tabs, masks = [], [], [], []
    for c in range(NCORES):
        s = c * TLOC
        if c == 0:
            xs = np.concatenate([x[:, 0:TILE, :], x[:, 0:TLOC, :]], axis=1)
            tb = np.concatenate([tab_full[0:TILE], tab_full[0:TLOC]], axis=0)
        else:
            xs = x[:, s - TILE:s + TLOC, :]
            tb = tab_full[s - TILE:s + TLOC]
        xtm.append(np.ascontiguousarray(xs))
        xcm.append(np.ascontiguousarray(xs.transpose(0, 2, 1)))
        tabs.append(np.ascontiguousarray(tb))
        masks.append(np.full((128, 1), 0.0 if c == 0 else 1.0, np.float32))

    nc_a = build_a()
    in_a = [{"xcm": xcm[c], "tab": tabs[c], "wk": wkT, "wv": wvT,
             "bias3": bias3} for c in range(NCORES)]
    res_a = run_bass_kernel_spmd(nc_a, in_a, core_ids=list(range(NCORES)))

    kv_all = np.ascontiguousarray(
        np.stack([res_a.results[c]["kvp"] for c in range(NCORES)]))
    ks_all = np.ascontiguousarray(
        np.stack([res_a.results[c]["ksp"][0] for c in range(NCORES)]))

    nc_b = build_b()
    in_b = [{"xcm": xcm[c], "xtm": xtm[c], "tab": tabs[c], "wq": wqT,
             "wr": wrT, "wkf": wkfT, "wvf": wvfT, "bias3": bias3, "gb": gbv,
             "mu2": mu2, "mask": masks[c], "ident": ident,
             "kv_all": kv_all, "ks_in": ks_all} for c in range(NCORES)]
    res_b = run_bass_kernel_spmd(nc_b, in_b, core_ids=list(range(NCORES)))

    out = np.concatenate([res_b.results[c]["y"] for c in range(NCORES)],
                         axis=1)
    return out
